# version 11
# baseline (speedup 1.0000x reference)
import contextlib
import numpy as np
import ml_dtypes
import concourse.bass as bass
import concourse.mybir as mybir
from concourse.bass_utils import run_bass_kernel_spmd

F32 = mybir.dt.float32
BF16 = mybir.dt.bfloat16
I32 = mybir.dt.int32
AF = mybir.ActivationFunctionType
ALU = mybir.AluOpType
AXX = mybir.AxisListType.X

T, DM, II = 4096, 1024, 2816
TS = 128
KC_D = DM // 128
KC_I = II // 128
NCH = 6
MA = float(1.5 * 2**23 * 0.5)
MB = float(1.5 * 2**23 * 1.0)
MA1 = 1.0 - MA
MBN = -(MB + 2.0)
MAGIC = float(0x5F3759E0)
EPS = 1e-5
INV_D = float(np.float32(1.0) / np.float32(DM))
INV6 = float(np.float32(1.0) / np.float32(6.0))
NEWTON = 2


class Sched:
    ENGS = ("PE", "DV", "AC", "GP", "SQ")

    def __init__(self):
        self.ops = []

    def add(self, eng, fn, reads=(), writes=(), dma=None, tag="", hz=False,
            dma_final=False):
        self.ops.append(dict(eng=eng, fn=fn, reads=tuple(reads),
                             writes=tuple(writes), dma=dma, tag=tag, hz=hz,
                             dma_final=dma_final))

    def barrier(self):
        self.ops.append(dict(eng="BAR", fn=None, reads=(), writes=(),
                             dma=None, tag="barrier"))

    def resolve(self):
        last_w, readers = {}, {}
        bar_set = set()
        last_of_stream = {}
        last_of_dmasem = {}
        for i, op in enumerate(self.ops):
            if op["eng"] == "BAR":
                bar_set = set(last_of_stream.values()) | set(last_of_dmasem.values())
                last_w, readers = {}, {}
                op["deps"] = set()
                continue
            deps = set(bar_set)
            for r in op["reads"]:
                w = last_w.get(r)
                if w is not None:
                    deps.add(w)
            for w_ in op["writes"]:
                w = last_w.get(w_)
                if w is not None:
                    deps.add(w)
                for rd in readers.get(w_, ()):
                    deps.add(rd)
            deps.discard(i)
            op["deps"] = deps
            for r in op["reads"]:
                readers.setdefault(r, []).append(i)
            for w_ in op["writes"]:
                last_w[w_] = i
                readers[w_] = []
            last_of_stream[op["eng"]] = i
            if op["dma"]:
                last_of_dmasem[op["dma"]] = i
        marked = set()
        for op in self.ops:
            if op["eng"] == "BAR":
                continue
            for d in op["deps"]:
                dop = self.ops[d]
                if (dop["dma"] or dop["eng"] != op["eng"] or dop["hz"]
                        or op["hz"]):
                    marked.add(d)
        cnt = {e: 0 for e in self.ENGS}
        qcnt = {}
        val = {}
        for i, op in enumerate(self.ops):
            if op["eng"] == "BAR":
                continue
            if op["dma"]:
                qcnt[op["dma"]] = qcnt.get(op["dma"], 0) + 16
                val[i] = (op["dma"], qcnt[op["dma"]])
                op["inc"] = (op["dma"], 16)
            elif i in marked:
                cnt[op["eng"]] += 1
                val[i] = ("s" + op["eng"], cnt[op["eng"]])
                op["inc"] = ("s" + op["eng"], 1)
            else:
                op["inc"] = None
        val_true = dict(val)
        for i, op in enumerate(self.ops):
            if op["eng"] != "BAR" and op["dma"] and op.get("dma_final"):
                val[i] = (op["dma"], qcnt[op["dma"]])
        watermark = {}
        for i, op in enumerate(self.ops):
            if op["eng"] == "BAR":
                continue
            waits = {}
            for d in sorted(op["deps"]):
                dop = self.ops[d]
                if d not in val:
                    continue
                if (dop["eng"] == op["eng"] and not dop["dma"]
                        and not dop["hz"] and not op["hz"]):
                    continue
                sem, v = val[d]
                if (dop["dma"] and dop.get("dma_final")
                        and op["dma"] == dop["dma"]):
                    sem, v = val_true[d]
                key = (op["eng"], sem)
                if watermark.get(key, -1) >= v:
                    continue
                waits[sem] = max(waits.get(sem, 0), v)
            for sem, v in waits.items():
                watermark[(op["eng"], sem)] = v
            op["waits"] = sorted(waits.items())
        self.final_q = dict(qcnt)
        chk = {}
        for i, op in enumerate(self.ops):
            if op["eng"] == "BAR":
                continue
            for sem, v in op["waits"]:
                assert v <= chk.get(sem, 0), (
                    f"wait-on-future: op {i} {op['tag']} {sem}>={v}")
            if op["inc"]:
                sem, n = op["inc"]
                chk[sem] = chk.get(sem, 0) + n

    def emit(self, sems):
        def run(stream, engobj):
            for op in self.ops:
                if op["eng"] != stream:
                    continue
                for sem, v in op["waits"]:
                    engobj.wait_ge(sems[sem], v)
                inst = op["fn"](engobj)
                if op["inc"] is not None:
                    sem, n = op["inc"]
                    inst.then_inc(sems[sem], n)
        return run


def build_nc(nstep=T // TS, hz_all=False):
    assert nstep % 2 == 0
    nc = bass.Bass("TRN2", target_bir_lowering=False)

    x_h = nc.dram_tensor("x", [T, DM], F32, kind="ExternalInput").ap()
    out_h = nc.dram_tensor("out", [T, DM], F32, kind="ExternalOutput").ap()
    wz_h = nc.dram_tensor("wz", [DM, DM], F32, kind="ExternalInput").ap()
    wh_h = nc.dram_tensor("wh", [DM, DM], F32, kind="ExternalInput").ap()
    wg_h = nc.dram_tensor("wg", [II, DM], F32, kind="ExternalInput").ap()
    wu_h = nc.dram_tensor("wu", [II, DM], F32, kind="ExternalInput").ap()
    wd_h = nc.dram_tensor("wd", [DM, II], F32, kind="ExternalInput").ap()
    bz_h = nc.dram_tensor("bz", [1, DM], F32, kind="ExternalInput").ap()
    bh_h = nc.dram_tensor("bh", [1, DM], F32, kind="ExternalInput").ap()
    gam_h = nc.dram_tensor("gam", [128, DM], F32, kind="ExternalInput").ap()
    bet_h = nc.dram_tensor("bet", [128, DM], F32, kind="ExternalInput").ap()
    idb_h = nc.dram_tensor("idb", [128, 128], BF16, kind="ExternalInput").ap()
    idf_h = nc.dram_tensor("idf", [128, 128], F32, kind="ExternalInput").ap()
    wdq_h = nc.dram_tensor("wdq", [II, DM], BF16, kind="Internal").ap()
    wzq_h = nc.dram_tensor("wzq", [DM, DM], BF16, kind="Internal").ap()
    whq_h = nc.dram_tensor("whq", [DM, DM], BF16, kind="Internal").ap()

    ctx = contextlib.ExitStack()
    sb = lambda n, s, d=F32: ctx.enter_context(nc.sbuf_tensor(n, s, d))
    ps = lambda n, s, d=F32: ctx.enter_context(nc.psum_tensor(n, s, d))

    WG = sb("WG", [128, KC_D * II], BF16)
    WU = sb("WU", [128, KC_D * II], BF16)
    SX = [sb("SX0", [128, DM]), sb("SX1", [128, DM])]
    U = [sb("U0", [128, DM]), sb("U1", [128, DM])]
    W = [sb("W0", [128, DM]), sb("W1", [128, DM])]
    XM = [sb("XM0", [128, DM]), sb("XM1", [128, DM])]
    AAR = sb("AAR", [128, 2048])
    BAR = sb("BAR", [128, 3072])
    CAR = sb("CAR", [128, 3072])
    GAMF = sb("GAMF", [128, DM])
    BETF = sb("BETF", [128, DM])
    JNK = [sb("JNK0", [128, 512]), sb("JNK1", [128, 512])]
    SGN = [sb("SGNs", [128, DM], BF16)] * 2
    SGNH = sb("SGNH", [128, 512], BF16)
    QXB = [sb("QXBs", [128, DM], BF16)] * 2
    XQF = [sb("XQF0", [128, DM], BF16), sb("XQF1", [128, DM], BF16)]
    XMF = sb("XMF", [128, DM], BF16)
    HQ = sb("HQ", [128, 512], BF16)
    HQF = [sb("HQF0", [128, 512], BF16), sb("HQF1", [128, 512], BF16)]
    WDB = [sb("WDB0", [128, 2 * DM], BF16), sb("WDB1", [128, 2 * DM], BF16)]
    WZB = [sb("WZB0", [128, DM], BF16), sb("WZB1", [128, DM], BF16)]
    WHB = [sb("WHB0", [128, DM], BF16), sb("WHB1", [128, DM], BF16)]
    ST = [sb("STg0", [128, 24]), sb("STg1", [128, 24])]
    STI = [sb("STIg0", [128, 1], I32), sb("STIg1", [128, 1], I32)]
    STB = [sb("STBg0", [128, 64]), sb("STBg1", [128, 64])]
    STB2 = [sb("STB2g0", [128, 64]), sb("STB2g1", [128, 64])]
    STR = [sb("STRg0", [128, 64]), sb("STRg1", [128, 64])]
    STR6 = [sb("STR6g0", [128, 64]), sb("STR6g1", [128, 64])]
    STS = [sb("STSg0", [128, 64]), sb("STSg1", [128, 64])]
    CARRY = sb("CARRY", [128, KC_D])
    EPST = sb("EPST", [128, 1])
    KON = sb("KON", [128, 3])
    ONESB = sb("ONESB", [1, 128], BF16)
    BZR = sb("BZR", [1, DM], BF16)
    BHR = sb("BHR", [1, DM], BF16)
    IDB = sb("IDB", [128, 128], BF16)
    IDF = sb("IDF", [128, 128], F32)

    PG = ps("PG", [128, 512])
    PU = ps("PU", [128, 512])
    PD0 = ps("PD0", [128, 512])
    PD1 = ps("PD1", [128, 512])
    PTR = ps("PTR", [128, 512])
    PTR2 = ps("PTR2", [128, 512])
    PTRB = PTR[:].bitcast(BF16)
    PTR2B = PTR2[:].bitcast(BF16)

    STNAMES = ["sum", "ssq", "negm", "m", "va", "r1", "halfr1",
               "sdf", "rstd", "nt", "nt2", "m2e", "su0", "su1",
               "sq0", "sq1", "y1"]
    STCOL = []
    for g in range(2):
        STCOL.append({nm: ST[g][:, i_:i_ + 1] for i_, nm in enumerate(STNAMES)})

    sch = Sched()

    def dma(q, dst, src, key_w, keys_r=(), tag="", sem=None, final=False):
        qsem = sem if sem is not None else ("qS" if q == "SQ" else "qG")
        sch.add(q, lambda e, d=dst, s=src: e.dma_start(out=d, in_=s),
                reads=keys_r, writes=(key_w,) if isinstance(key_w, str) else key_w,
                dma=qsem, tag=tag, dma_final=final)

    def rsqrt_chain(g, prefix):
        C = STCOL[g]
        va, sdf, rstd, nt = C["va"], C["sdf"], C["rstd"], C["nt"]
        nt2, y1 = C["nt2"], C["y1"]
        stk = f"ST{g}"
        sch.add("DV", lambda v, a=va, o=sdf: v.tensor_scalar(
            o, a.bitcast(I32), -0.5, MAGIC, ALU.mult, ALU.add),
            reads=[stk], writes=[stk], tag=prefix + "seed", hz=True)
        sch.add("AC", lambda s, i=sdf, g_=g: s.copy(STI[g_][:], i),
                reads=[stk], writes=[f"STI{g}"], tag=prefix + "f2i")
        sch.add("DV", lambda v, g_=g, o=rstd: v.tensor_copy(
            o, STI[g_][:].bitcast(F32)),
            reads=[f"STI{g}"], writes=[stk], tag=prefix + "i2f", hz=True)
        ys = [rstd, y1]
        assert NEWTON % 2 == 0
        for it in range(NEWTON):
            src, dst_ = ys[it % 2], ys[(it + 1) % 2]
            sch.add("DV", lambda v, a=src: v.tensor_tensor(nt, a, a, ALU.mult),
                    reads=[stk], writes=[stk], hz=True)
            sch.add("DV", lambda v: v.tensor_tensor(nt2, nt, va, ALU.mult),
                    reads=[stk], writes=[stk], hz=True)
            sch.add("DV", lambda v: v.tensor_scalar(nt, nt2, -0.5, 1.5,
                                                    ALU.mult, ALU.add),
                    reads=[stk], writes=[stk], hz=True)
            sch.add("DV", lambda v, a=src, o=dst_: v.tensor_tensor(
                o, a, nt, ALU.mult), reads=[stk], writes=[stk], hz=True)

    def ln_stats(g, src, src_key, junk, junk_key, prefix, need_m=False,
                 need_r1=False, sum_ready=False):
        C = STCOL[g]
        stk = f"ST{g}"
        if not sum_ready:
            sch.add("DV", lambda v, s=src: v.tensor_reduce(
                C["sum"], s, AXX, ALU.add),
                reads=[src_key], writes=[stk], tag=prefix + "sum", hz=True)
        sch.add("AC", lambda s, i=src, j=junk: s.activation(
            j[:, 0:512], i[:, 0:512], AF.Square, accum_out=C["sq0"]),
            reads=[src_key], writes=[junk_key, stk], tag=prefix + "ssq0")
        sch.add("AC", lambda s, i=src, j=junk: s.activation(
            j[:, 0:512], i[:, 512:1024], AF.Square, accum_out=C["sq1"]),
            reads=[src_key], writes=[junk_key, stk], tag=prefix + "ssq1")
        sch.add("DV", lambda v: v.tensor_tensor(
            C["ssq"], C["sq0"], C["sq1"], ALU.add),
            reads=[stk], writes=[stk], tag=prefix + "ssqsum", hz=True)
        sch.add("DV", lambda v: v.tensor_scalar(
            C["negm"], C["sum"], -INV_D, None, ALU.mult),
            reads=[stk], writes=[stk], hz=True)
        if need_m:
            sch.add("DV", lambda v: v.tensor_scalar(
                C["m"], C["sum"], INV_D, None, ALU.mult),
                reads=[stk], writes=[stk], hz=True)
        sch.add("DV", lambda v: v.scalar_tensor_tensor(
            C["m2e"], C["negm"], C["negm"], EPST[:],
            ALU.mult, ALU.subtract),
            reads=[stk, "EPST"], writes=[stk], hz=True)
        sch.add("DV", lambda v: v.scalar_tensor_tensor(
            C["va"], C["ssq"], INV_D, C["m2e"],
            ALU.mult, ALU.subtract),
            reads=[stk], writes=[stk], hz=True)
        if need_r1:
            sch.add("DV", lambda v: v.reciprocal(C["r1"], C["va"]),
                    reads=[stk], writes=[stk], hz=True)
            sch.add("DV", lambda v: v.tensor_scalar(
                C["halfr1"], C["r1"], 0.5, None, ALU.mult),
                reads=[stk], writes=[stk], hz=True)
        rsqrt_chain(g, prefix)

    def quant(g, src, src_key, dst, dst_key, w, tmp1, tmp2, tmp3, tmp4, keys4,
              sgn_view, sgn_key, nblk_off, use_negm, prefix):
        C = STCOL[g]
        stk = f"ST{g}"
        k1, k2, k3, k4 = keys4
        nb = w // 16
        bmv = STB[g][:, nblk_off:nblk_off + nb]
        bm2 = STB2[g][:, nblk_off:nblk_off + nb]
        rrv = STR[g][:, nblk_off:nblk_off + nb]
        rr6 = STR6[g][:, nblk_off:nblk_off + nb]
        ssv = STS[g][:, nblk_off:nblk_off + nb]
        negm = C["negm"]
        if use_negm:
            sch.add("AC", lambda s, i=src, o=tmp1: s.activation(
                o, i, AF.Abs, bias=negm), reads=[src_key, stk],
                writes=[k1], tag=prefix + "abs")
        else:
            sch.add("AC", lambda s, i=src, o=tmp1: s.activation(o, i, AF.Abs),
                    reads=[src_key], writes=[k1], tag=prefix + "abs")
        sch.add("DV", lambda v, i=tmp1, o=bmv, nb_=nb: v.tensor_reduce(
            o, i.rearrange("p (b i) -> p b i", i=16), AXX, ALU.max),
            reads=[k1], writes=[f"STB{g}"], tag=prefix + "bmax", hz=True)
        sch.add("DV", lambda v, i=bmv, o=bm2: v.tensor_scalar(
            o, i, 6e-12, None, ALU.max), reads=[f"STB{g}"],
            writes=[f"STB2{g}"], hz=True)
        sch.add("DV", lambda v, i=bm2, o=rrv: v.reciprocal(o, i),
                reads=[f"STB2{g}"], writes=[f"STR{g}"], hz=True)
        sch.add("DV", lambda v, i=rrv, o=rr6: v.tensor_scalar(
            o, i, 6.0, None, ALU.mult), reads=[f"STR{g}"],
            writes=[f"STR6{g}"], hz=True)
        sch.add("DV", lambda v, i=bm2, o=ssv: v.tensor_scalar(
            o, i, INV6, None, ALU.mult), reads=[f"STB2{g}"],
            writes=[f"STS{g}"], hz=True)
        sch.add("DV", lambda v, i=tmp1, o=tmp2, r=rr6, nb_=nb: v.tensor_tensor(
            o.rearrange("p (b i) -> p b i", i=16),
            i.rearrange("p (b i) -> p b i", i=16),
            r.unsqueeze(2).broadcast_to([128, nb_, 16]), ALU.mult),
            reads=[k1, f"STR6{g}"], writes=[k2], tag=prefix + "xn")
        if use_negm:
            sch.add("AC", lambda s, i=src, o=sgn_view: s.activation(
                o, i, AF.Sign, bias=negm), reads=[src_key, stk],
                writes=[sgn_key])
        else:
            sch.add("AC", lambda s, i=src, o=sgn_view: s.activation(
                o, i, AF.Sign), reads=[src_key], writes=[sgn_key])
        sch.add("AC", lambda s, i=tmp2, o=tmp3: s.activation(
            o, i, AF.Abs, bias=KON[:, 0:1]), reads=[k2, "KON"], writes=[k3],
            tag=prefix + "rA")
        sch.add("DV", lambda v, i=tmp3, o=tmp4: v.tensor_scalar(
            o, i, MA1, 3.0, ALU.add, ALU.min), reads=[k3], writes=[k4])
        sch.add("AC", lambda s, i=tmp2, o=tmp1: s.activation(
            o, i, AF.Abs, bias=KON[:, 1:2]), reads=[k2, "KON"], writes=[k1],
            tag=prefix + "rB")
        sch.add("DV", lambda v, i=tmp1, o=tmp3: v.tensor_scalar(
            o, i, MBN, 0.0, ALU.add, ALU.max), reads=[k1], writes=[k3])
        sch.add("AC", lambda s, i=tmp2, o=tmp1: s.activation(
            o, i, AF.Sign, bias=KON[:, 2:3]), reads=[k2, "KON"], writes=[k1],
            tag=prefix + "s5")
        sch.add("DV", lambda v, a=tmp3, b=tmp4, o=tmp2: v.scalar_tensor_tensor(
            o, a, 2.0, b, ALU.min, ALU.add), reads=[k3, k4], writes=[k2])
        sch.add("DV", lambda v, a=tmp2, b=tmp1, o=tmp4: v.tensor_tensor(
            o, a, b, ALU.add), reads=[k2, k1], writes=[k4])
        sch.add("DV", lambda v, i=tmp4, o=tmp2, s_=ssv, nb_=nb: v.tensor_tensor(
            o.rearrange("p (b i) -> p b i", i=16),
            i.rearrange("p (b i) -> p b i", i=16),
            s_.unsqueeze(2).broadcast_to([128, nb_, 16]), ALU.mult),
            reads=[k4, f"STS{g}"], writes=[k2])
        sch.add("DV", lambda v, i=tmp2, o=dst, sg=sgn_view: v.tensor_tensor(
            o, i, sg, ALU.mult), reads=[k2, sgn_key], writes=[dst_key],
            tag=prefix + "deq")

    tr_bank = [0]

    def transpose_to(src_bf, src_key, dst, dst_key, nchunks, dst_col_of,
                     prefix, dtype="bf"):
        per_bank = 8 if dtype == "bf" else 4
        ident = IDB if dtype == "bf" else IDF
        for kc in range(nchunks):
            bi = (tr_bank[0] + kc // per_bank) % 2
            pview = (PTRB if bi == 0 else PTR2B) if dtype == "bf" else                 (PTR, PTR2)[bi][:]
            slot = kc % per_bank
            pkey = "PTR" if bi == 0 else "PTR2"
            sch.add("PE", lambda t, i=src_bf, kc_=kc, pv=pview, sl=slot,
                    idn=ident: t.transpose(
                        pv[:, sl * 128:(sl + 1) * 128],
                        i[:, kc_ * 128:(kc_ + 1) * 128], idn[:]),
                    reads=[src_key, "IDENT"], writes=[pkey],
                    tag=f"{prefix}tr{kc}")
            if slot == per_bank - 1 or kc == nchunks - 1:
                nblks = slot + 1
                c0 = dst_col_of(kc - (nblks - 1))
                sch.add("AC", lambda s, pv=pview, nb_=nblks, c0_=c0, d=dst:
                        s.copy(d[:, c0_:c0_ + nb_ * 128], pv[:, 0:nb_ * 128]),
                        reads=[pkey], writes=[dst_key], tag=f"{prefix}ev{kc}")
        tr_bank[0] = (tr_bank[0] + (nchunks + per_bank - 1) // per_bank) % 2

    for dst_, src_, kk_ in [(IDB[:], idb_h, "IDENT"), (IDF[:], idf_h, "IDENT2"),
                            (GAMF[:], gam_h, "GAMF"), (BETF[:], bet_h, "BETF"),
                            (CAR[0:1, 0:DM], bz_h, "CC"),
                            (CAR[0:1, DM:2 * DM], bh_h, "CC")]:
        dma("SQ", dst_, src_, kk_, sem="qC", final=True, tag="ldc")
    sch.add("GP", lambda g: g.memset(ONESB[:], 1.0), writes=["ONESB"])
    sch.add("GP", lambda g: g.memset(CARRY[:], 0.0), writes=["CARRY"])
    sch.add("GP", lambda g: g.memset(EPST[:], EPS), writes=["EPST"])
    sch.add("GP", lambda g: g.memset(KON[:, 0:1], MA), writes=["KON"])
    sch.add("GP", lambda g: g.memset(KON[:, 1:2], MB), writes=["KON"])
    sch.add("GP", lambda g: g.memset(KON[:, 2:3], -5.0), writes=["KON"])
    sch.add("AC", lambda s: s.copy(BZR[:], CAR[0:1, 0:DM]),
            reads=["CC"], writes=["BZR"])
    sch.add("AC", lambda s: s.copy(BHR[:], CAR[0:1, DM:2 * DM]),
            reads=["CC"], writes=["BHR"])

    def prep_gateup(w_hbm, dstW, dstW_key):
        for ch in range(KC_I):
            stg = CAR[:, (ch % 2) * DM:(ch % 2) * DM + DM]
            dma("SQ", stg, w_hbm[ch * 128:(ch + 1) * 128, :], "CC",
                tag=f"ldw{ch}", sem=f"qP{ch % 2}")
            quant(0, stg, "CC", QXB[0][:], "QXBs", DM,
                  AAR[:, 0:DM], AAR[:, DM:2 * DM], BAR[:, 0:DM],
                  BAR[:, DM:2 * DM], ("AA", "AA", "BB", "BB"),
                  SGN[0][:, 0:DM], "SGNs", 0,
                  use_negm=False, prefix=f"pw{ch}_")
            for kcd in range(KC_D):
                pview = PTRB if kcd % 2 == 0 else PTR2B
                pkey = "PTR" if kcd % 2 == 0 else "PTR2"
                sch.add("PE", lambda t, kcd_=kcd, pv=pview: t.transpose(
                    pv[:, 0:128], QXB[0][:, kcd_ * 128:(kcd_ + 1) * 128],
                    IDB[:]), reads=["QXBs", "IDENT"], writes=[pkey])
                sch.add("AC", lambda s, kcd_=kcd, ch_=ch, pv=pview, d=dstW:
                        s.copy(d[:, kcd_ * II + ch_ * 128:
                                 kcd_ * II + ch_ * 128 + 128], pv[:, 0:128]),
                        reads=[pkey], writes=[dstW_key])

    prep_gateup(wg_h, WG, "WG")
    prep_gateup(wu_h, WU, "WU")

    for ch in range(KC_D):
        pieces = [(0, 1024), (1024, 1024), (2048, 768)]
        for pi, (plo, pw) in enumerate(pieces):
            stg = CAR[:, plo:plo + pw]
            dma("SQ", stg, wd_h[ch * 128:(ch + 1) * 128, plo:plo + pw], "CC",
                tag=f"ldwd{ch}_{pi}", sem=f"qP{(ch * 3 + pi) % 2}")
            quant(0, stg, "CC", QXB[0][:, 0:pw], "QXBs", pw,
                  AAR[:, 0:pw], AAR[:, 1024:1024 + pw], BAR[:, 0:pw],
                  BAR[:, 1024:1024 + pw], ("AA", "AA", "BB", "BB"),
                  SGN[0][:, 0:pw], "SGNs", 0,
                  use_negm=False, prefix=f"pd{ch}_{pi}_")
            for kk in range(pw // 128):
                kg = plo // 128 + kk
                pview = PTRB if kg % 2 == 0 else PTR2B
                pkey = "PTR" if kg % 2 == 0 else "PTR2"
                sch.add("PE", lambda t, kk_=kk, pv=pview: t.transpose(
                    pv[:, 0:128], QXB[0][:, kk_ * 128:(kk_ + 1) * 128],
                    IDB[:]), reads=["QXBs", "IDENT"], writes=[pkey])
                bst = BAR[:, 2048 + (kg % 2) * 64:
                          2048 + (kg % 2) * 64 + 64].bitcast(BF16)
                sch.add("AC", lambda s, pv=pview, b=bst: s.copy(
                    b[:, 0:128], pv[:, 0:128]), reads=[pkey], writes=["BB"])
                dma("GP", wdq_h[kg * 128:(kg + 1) * 128,
                                ch * 128:(ch + 1) * 128],
                    bst[:, 0:128], "WDQH", keys_r=["BB"],
                    tag=f"stwd{ch}_{kg}", sem=f"qW{kg % 2}")

    for w_hbm, wq_hbm, nm in ((wz_h, wzq_h, "z"), (wh_h, whq_h, "h")):
        for ch in range(KC_D):
            stg = CAR[:, (ch % 2) * DM:(ch % 2) * DM + DM]
            dma("SQ", stg, w_hbm[ch * 128:(ch + 1) * 128, :], "CC",
                tag=f"ldw{nm}{ch}", sem=f"qP{ch % 2}")
            bst = BAR[:, 0:512].bitcast(BF16) if ch % 2 == 0 else \
                BAR[:, 512:1024].bitcast(BF16)
            sch.add("AC", lambda s, st_=stg, b=bst: s.copy(b[:, 0:DM], st_),
                    reads=["CC"], writes=["BB"])
            dma("GP", wq_hbm[ch * 128:(ch + 1) * 128, :], bst[:, 0:DM],
                "WQH" + nm, keys_r=["BB"], tag=f"stw{nm}{ch}",
                sem=f"qW{ch % 2}")

    sch.barrier()

    def stageA(j):
        g = j % 2
        sxj, sxk = SX[g], f"SX{g}"
        pfx = f"s{j}_"
        ln_stats(g, sxj[:], sxk, JNK[g][:], f"JNK{g}", pfx + "ln1_",
                 need_r1=True)
        for hh in range(2):
            hs = hh * 512
            quant(g, sxj[:, hs:hs + 512], sxk, QXB[g][:, hs:hs + 512],
                  "QXBs", 512, AAR[:, 0:512], AAR[:, 512:1024],
                  AAR[:, 1024:1536], AAR[:, 1536:2048],
                  ("AA", "AA", "AA", "AA"), SGN[g][:, hs:hs + 512],
                  "SGNs", 32 * hh, use_negm=True, prefix=pfx + f"qx{hh}_")
        transpose_to(QXB[g][:], "QXBs", XQF[g], f"XQF{g}", KC_D,
                     lambda kc: kc * 128, pfx + "qxt_")

    def quant_w(j):
        g = j % 2
        pfx = f"s{j}_"
        ln_stats(g, W[g][:], f"W{g}", JNK[g][:], f"JNK{g}", pfx + "ln3_",
                 need_r1=True, sum_ready=True)
        for hh in range(2):
            hs = hh * 512
            quant(g, W[g][:, hs:hs + 512], f"W{g}", QXB[g][:, hs:hs + 512],
                  "QXBs", 512, AAR[:, 0:512], AAR[:, 512:1024],
                  AAR[:, 1024:1536], AAR[:, 1536:2048],
                  ("AA", "AA", "AA", "AA"), SGN[g][:, hs:hs + 512],
                  "SGNs", 32 * hh, use_negm=True, prefix=pfx + f"qw{hh}_")
        transpose_to(QXB[g][:], "QXBs", XQF[g], f"XQF{g}", KC_D,
                     lambda kc: kc * 128, pfx + "qwt_")

    def ffn(j, res, res_key, dst, dst_key, prefix):
        g = j % 2
        C = STCOL[g]
        stk = f"ST{g}"
        xqk = f"XQF{g}"
        npair = (KC_I + 1) // 2

        def emit_wd_load(pr):
            if pr >= npair:
                return
            k0 = pr * 2
            nk = min(2, KC_I - k0)
            wb = WDB[pr % 2]
            wbk = f"WDB{pr % 2}"
            dst_ = wb[:].rearrange("p (kk n) -> p kk n", n=DM)[:, 0:nk, :]
            src_ = wdq_h[k0 * 128:(k0 + nk) * 128, :].rearrange(
                "(kk p) n -> p kk n", p=128)
            dma("SQ", dst_, src_, wbk, tag=f"{prefix}ldd{pr}",
                sem=f"qD{pr % 2}")

        emit_wd_load(0)
        emit_wd_load(1)
        next_pair = [2]
        for c in range(NCH):
            lo = c * 512
            w = min(512, II - lo)
            nkk = w // 128
            while next_pair[0] <= 2 * c + 1 and next_pair[0] < npair:
                emit_wd_load(next_pair[0])
                next_pair[0] += 1
            for kc in range(KC_D):
                sch.add("PE", lambda t, kc_=kc, lo_=lo, w_=w, g_=g: t.matmul(
                    PG[:, 0:w_], XQF[g_][:, kc_ * 128:(kc_ + 1) * 128],
                    WG[:, kc_ * II + lo_:kc_ * II + lo_ + w_],
                    start=(kc_ == 0), stop=(kc_ == KC_D - 1)),
                    reads=[xqk, "WG"], writes=["PG"], tag=f"{prefix}g{c}")
            sch.add("AC", lambda s, w_=w: s.activation(
                BAR[:, 0:w_], PG[:, 0:w_], AF.Sigmoid, scale=C["rstd"]),
                reads=["PG", stk], writes=["BB"], tag=f"{prefix}sg{c}")
            for kc in range(KC_D):
                sch.add("PE", lambda t, kc_=kc, lo_=lo, w_=w, g_=g: t.matmul(
                    PU[:, 0:w_], XQF[g_][:, kc_ * 128:(kc_ + 1) * 128],
                    WU[:, kc_ * II + lo_:kc_ * II + lo_ + w_],
                    start=(kc_ == 0), stop=(kc_ == KC_D - 1)),
                    reads=[xqk, "WU"], writes=["PU"], tag=f"{prefix}u{c}")
            sch.add("DV", lambda v, w_=w: v.tensor_tensor(
                BAR[:, 512:512 + w_], BAR[:, 0:w_], PG[:, 0:w_], ALU.mult),
                reads=["BB", "PG"], writes=["BB"], tag=f"{prefix}th{c}")
            sch.add("DV", lambda v, w_=w: v.tensor_tensor(
                BAR[:, 1024:1024 + w_], BAR[:, 512:512 + w_], PU[:, 0:w_],
                ALU.mult), reads=["BB", "PU"], writes=["BB"],
                tag=f"{prefix}h{c}")
            quant(g, BAR[:, 1024:1024 + w], "BB", HQ[:, 0:w], "HQ", w,
                  BAR[:, 1536:1536 + w], BAR[:, 2048:2048 + w],
                  BAR[:, 2560:2560 + w], BAR[:, 0:w],
                  ("BB", "BB", "BB", "BB"), SGNH[:, 0:w], "SGNH", 0,
                  use_negm=False, prefix=f"{prefix}qh{c}_")
            hf = HQF[c % 2]
            hfk = f"HQF{c % 2}"
            hbi = tr_bank[0]
            tr_bank[0] ^= 1
            hview = PTRB if hbi == 0 else PTR2B
            hpkey = "PTR" if hbi == 0 else "PTR2"
            for kk in range(nkk):
                sch.add("PE", lambda t, kk_=kk, pv=hview: t.transpose(
                    pv[:, kk_ * 128:(kk_ + 1) * 128],
                    HQ[:, kk_ * 128:(kk_ + 1) * 128], IDB[:]),
                    reads=["HQ", "IDENT"], writes=[hpkey])
            sch.add("AC", lambda s, pv=hview, d=hf, n_=nkk: s.copy(
                d[:, 0:n_ * 128], pv[:, 0:n_ * 128]),
                reads=[hpkey], writes=[hfk])
            for kk in range(nkk):
                kg = 4 * c + kk
                pr = kg // 2
                wb = WDB[pr % 2]
                wbk = f"WDB{pr % 2}"
                off = (kg % 2) * DM
                sch.add("PE", lambda t, kk_=kk, kg_=kg, wb_=wb, hf_=hf, o_=off:
                        t.matmul(
                            PD0[:, 0:512], hf_[:, kk_ * 128:(kk_ + 1) * 128],
                            wb_[:, o_:o_ + 512], start=(kg_ == 0),
                            stop=(kg_ == KC_I - 1)),
                        reads=[hfk, wbk], writes=["PD0"],
                        tag=f"{prefix}d0_{kg}")
                sch.add("PE", lambda t, kk_=kk, kg_=kg, wb_=wb, hf_=hf, o_=off:
                        t.matmul(
                            PD1[:, 0:512], hf_[:, kk_ * 128:(kk_ + 1) * 128],
                            wb_[:, o_ + 512:o_ + 1024], start=(kg_ == 0),
                            stop=(kg_ == KC_I - 1)),
                        reads=[hfk, wbk], writes=["PD1"],
                        tag=f"{prefix}d1_{kg}")
        sch.add("DV", lambda v: v.scalar_tensor_tensor(
            dst[:, 0:512], PD0[:], C["halfr1"], res[:, 0:512],
            ALU.mult, ALU.add, accum_out=C["su0"]),
            reads=["PD0", stk, res_key], writes=[dst_key, stk],
            tag=f"{prefix}ev0", hz=True)
        sch.add("DV", lambda v: v.scalar_tensor_tensor(
            dst[:, 512:1024], PD1[:], C["halfr1"], res[:, 512:1024],
            ALU.mult, ALU.add, accum_out=C["su1"]),
            reads=["PD1", stk, res_key], writes=[dst_key, stk],
            tag=f"{prefix}ev1", hz=True)
        sch.add("DV", lambda v: v.tensor_tensor(
            C["sum"], C["su0"], C["su1"], ALU.add),
            reads=[stk], writes=[stk], tag=f"{prefix}sumu", hz=True)

    def stageB(j):
        g = j % 2
        ffn(j, SX[g], f"SX{g}", U[g], f"U{g}", f"s{j}_f1_")

    def stageCD(j):
        g = j % 2
        C = STCOL[g]
        stk = f"ST{g}"
        pfx = f"s{j}_"
        ln_stats(g, U[g][:], f"U{g}", JNK[g][:], f"JNK{g}", pfx + "ln2_",
                 need_m=True, sum_ready=True)
        sch.add("DV", lambda v: v.tensor_scalar(
            XM[g][:], U[g][:], C["m"], C["rstd"], ALU.subtract, ALU.mult),
            reads=[f"U{g}", stk], writes=[f"XM{g}"], tag=pfx + "xm")
        sch.add("AC", lambda s: s.copy(QXB[g][:], XM[g][:]),
                reads=[f"XM{g}"], writes=["QXBs"], tag=pfx + "xmb")
        transpose_to(QXB[g][:], "QXBs", XMF, "XMF", KC_D,
                     lambda kc: kc * 128, pfx + "xmt_")

        def emit_zh_load(kc):
            if kc >= KC_D:
                return
            for wbuf, src_h, nm2, qn in ((WZB[kc % 2], wzq_h, "WZB", "qZ"),
                                         (WHB[kc % 2], whq_h, "WHB", "qH")):
                dma("SQ", wbuf[:], src_h[kc * 128:(kc + 1) * 128, :],
                    f"{nm2}{kc % 2}", tag=pfx + f"ld{nm2}{kc}",
                    sem=f"{qn}{kc % 2}")

        emit_zh_load(0)
        emit_zh_load(1)
        for kc in range(KC_D):
            wzb, wzk = WZB[kc % 2], f"WZB{kc % 2}"
            whb, whk = WHB[kc % 2], f"WHB{kc % 2}"
            off = 0
            for half, pk, pa, pkk, pak in ((0, PD0, PG, "PD0", "PG"),
                                           (1, PD1, PU, "PD1", "PU")):
                sch.add("PE", lambda t, kc_=kc, h_=half, w_=wzb, p_=pk, o_=off:
                        t.matmul(
                            p_[:], XMF[:, kc_ * 128:(kc_ + 1) * 128],
                            w_[:, o_ + h_ * 512:o_ + (h_ + 1) * 512],
                            start=(kc_ == 0), stop=False),
                        reads=["XMF", wzk], writes=[pkk],
                        tag=pfx + f"mk{kc}_{half}")
                sch.add("PE", lambda t, kc_=kc, h_=half, w_=whb, p_=pa, o_=off:
                        t.matmul(
                            p_[:], XMF[:, kc_ * 128:(kc_ + 1) * 128],
                            w_[:, o_ + h_ * 512:o_ + (h_ + 1) * 512],
                            start=(kc_ == 0), stop=False),
                        reads=["XMF", whk], writes=[pak],
                        tag=pfx + f"ma{kc}_{half}")
            emit_zh_load(kc + 2)
        for half, pk, pa, pkk, pak in ((0, PD0, PG, "PD0", "PG"),
                                       (1, PD1, PU, "PD1", "PU")):
            sch.add("PE", lambda t, h_=half, p_=pk: t.matmul(
                p_[:], ONESB[:, 0:128], BZR[:, h_ * 512:(h_ + 1) * 512],
                start=False, stop=True),
                reads=["ONESB", "BZR"], writes=[pkk], tag=pfx + f"bk{half}")
            sch.add("PE", lambda t, h_=half, p_=pa: t.matmul(
                p_[:], ONESB[:, 0:128], BHR[:, h_ * 512:(h_ + 1) * 512],
                start=False, stop=True),
                reads=["ONESB", "BHR"], writes=[pak], tag=pfx + f"ba{half}")
        allp = ["PD0", "PG", "PD1", "PU"]
        for half, pk, pa in ((0, PD0, PG), (1, PD1, PU)):
            hs = half * 512
            sch.add("AC", lambda s, p_=pk, hs_=hs: s.activation(
                AAR[:, hs_:hs_ + 512], p_[:], AF.Sigmoid, scale=-1.0),
                reads=allp, writes=["AA"], tag=pfx + f"c{half}")
            sch.add("AC", lambda s, p_=pk, hs_=hs: s.activation(
                BAR[:, hs_:hs_ + 512], p_[:], AF.Sigmoid),
                reads=allp, writes=["BB"], tag=pfx + f"z{half}")
            sch.add("AC", lambda s, p_=pa, hs_=hs: s.activation(
                BAR[:, 1024 + hs_:1024 + hs_ + 512], p_[:], AF.Sigmoid),
                reads=allp, writes=["BB"], tag=pfx + f"sa{half}")
            sch.add("DV", lambda v, p_=pa, hs_=hs: v.scalar_tensor_tensor(
                BAR[:, 2048 + hs_:2048 + hs_ + 512], p_[:], 0.5,
                BAR[:, 1024 + hs_:1024 + hs_ + 512], ALU.add, ALU.max),
                reads=allp + ["BB"], writes=["BB"], tag=pfx + f"g{half}")
        sch.add("GP", lambda g_: g_.tensor_tensor(
            AAR[:, 1024:2048], BAR[:, 0:1024], BAR[:, 2048:3072], ALU.mult),
            reads=["BB"], writes=["AA"], tag=pfx + "v")
        transpose_to(AAR[:, 0:1024], "AA", CAR, "CC", KC_D,
                     lambda kc: kc * 128, pfx + "ct_", dtype="f32")
        transpose_to(AAR[:, 1024:2048], "AA", CAR, "CC2", KC_D,
                     lambda kc: 1024 + kc * 128, pfx + "vt_", dtype="f32")
        for kc in range(KC_D):
            sch.add("DV", lambda v, kc_=kc: v.tensor_tensor_scan(
                CAR[:, 2048 + kc_ * 128:2048 + (kc_ + 1) * 128],
                CAR[:, kc_ * 128:(kc_ + 1) * 128],
                CAR[:, 1024 + kc_ * 128:1024 + (kc_ + 1) * 128],
                CARRY[:, kc_:kc_ + 1], ALU.mult, ALU.add),
                reads=["CC", "CC2", "CARRY"], writes=["CC3"],
                tag=pfx + f"scan{kc}", hz=True)
        sch.add("DV", lambda v: v.tensor_copy(
            CARRY[:, 0:KC_D].unsqueeze(2),
            CAR[:, 2048:3072].rearrange("p (c t) -> p c t", t=128)[:, :, 127:128]),
            reads=["CC3"], writes=["CARRY"], tag=pfx + "carry", hz=True)
        transpose_to(CAR[:, 2048:3072], "CC3", BAR, "BB", KC_D,
                     lambda kc: 2048 + kc * 128, pfx + "ht_", dtype="f32")
        ht = BAR[:, 2048:3072]
        ln_stats(g, ht, "BB", JNK[g][:], f"JNK{g}", pfx + "lnh_", need_m=True)
        sch.add("DV", lambda v: v.tensor_scalar(
            CAR[:, 0:1024], ht, C["m"], C["rstd"], ALU.subtract, ALU.mult),
            reads=["BB", stk], writes=["CC"], tag=pfx + "hn")
        sch.add("DV", lambda v: v.tensor_tensor(
            CAR[:, 1024:2048], CAR[:, 0:1024], GAMF[:], ALU.mult),
            reads=["CC", "GAMF"], writes=["CC2"], tag=pfx + "o1")
        sch.add("GP", lambda g_: g_.tensor_tensor(
            CAR[:, 0:1024], XM[g][:], BETF[:], ALU.add),
            reads=[f"XM{g}", "BETF"], writes=["CC"], tag=pfx + "tbx")
        sch.add("GP", lambda g_: g_.tensor_tensor(
            BAR[:, 0:1024], CAR[:, 1024:2048], CAR[:, 0:1024], ALU.add),
            reads=["CC", "CC2"], writes=["BB"], tag=pfx + "o2")
        sch.add("DV", lambda v: v.scalar_tensor_tensor(
            W[g][:, 0:512], BAR[:, 0:512], 1.0, U[g][:, 0:512],
            ALU.mult, ALU.add, accum_out=C["su0"]),
            reads=["BB", f"U{g}"], writes=[f"W{g}", stk],
            tag=pfx + "w0", hz=True)
        sch.add("DV", lambda v: v.scalar_tensor_tensor(
            W[g][:, 512:1024], BAR[:, 512:1024], 1.0, U[g][:, 512:1024],
            ALU.mult, ALU.add, accum_out=C["su1"]),
            reads=["BB", f"U{g}"], writes=[f"W{g}", stk],
            tag=pfx + "w1", hz=True)
        sch.add("DV", lambda v: v.tensor_tensor(
            C["sum"], C["su0"], C["su1"], ALU.add),
            reads=[stk], writes=[stk], tag=pfx + "wsum", hz=True)

    def stageE(j):
        g = j % 2
        quant_w(j)
        ffn(j, W[g], f"W{g}", SX[g], f"SX{g}", f"s{j}_f2_")
        dma("GP", out_h[j * 128:(j + 1) * 128, :], SX[g][:], f"OUTST{j}",
            keys_r=[f"SX{g}"], tag=f"s{j}_stout", sem=f"qO{g}")

    def prefetch_x(j):
        if j < nstep:
            dma("SQ", SX[j % 2][:], x_h[j * 128:(j + 1) * 128, :],
                f"SX{j % 2}", tag=f"s{j}_ldx", sem=f"qX{j % 2}")

    prefetch_x(0)
    prefetch_x(1)
    stageA(0)
    stageA(1)
    for p in range(nstep // 2):
        j0, j1 = 2 * p, 2 * p + 1
        stageB(j0)
        stageB(j1)
        stageCD(j0)
        stageCD(j1)
        stageE(j0)
        prefetch_x(j0 + 2)
        if j0 + 2 < nstep:
            stageA(j0 + 2)
        stageE(j1)
        prefetch_x(j1 + 2)
        if j1 + 2 < nstep:
            stageA(j1 + 2)

    if hz_all:
        for op in sch.ops:
            if op["eng"] != "BAR":
                op["hz"] = True
    sch.resolve()

    sems = {}
    semnames = set(["sPE", "sDV", "sAC", "sGP", "sSQ"])
    for op in sch.ops:
        if op["eng"] != "BAR" and op["dma"]:
            semnames.add(op["dma"])
    for s_ in sorted(semnames):
        sems[s_] = ctx.enter_context(nc.semaphore(s_))
    block = ctx.enter_context(nc.Block())
    run = sch.emit(sems)

    @block.tensor
    def _(tensor):
        run("PE", tensor)

    @block.vector
    def _(vector):
        run("DV", vector)

    @block.scalar
    def _(scalar):
        run("AC", scalar)

    @block.gpsimd
    def _(gpsimd):
        run("GP", gpsimd)
        for s_ in sorted(sch.final_q):
            gpsimd.wait_ge(sems[s_], sch.final_q[s_])

    @block.sync
    def _(sync):
        run("SQ", sync)

    ctx.close()
    return nc


_NC_CACHE = {}


def kernel(L0_attn_x, L0_attn_Wz, L0_attn_bz, L0_attn_Wh, L0_attn_bh,
           L0_attn_gamma, L0_attn_beta, L0_ffn_w_gate, L0_ffn_w_up,
           L0_ffn_w_down):
    B = L0_attn_x.shape[0]
    if "nc" not in _NC_CACHE:
        _NC_CACHE["nc"] = build_nc()
    nc = _NC_CACHE["nc"]
    bf = ml_dtypes.bfloat16
    common = dict(
        wz=np.ascontiguousarray(L0_attn_Wz, np.float32),
        wh=np.ascontiguousarray(L0_attn_Wh, np.float32),
        wg=np.ascontiguousarray(L0_ffn_w_gate, np.float32),
        wu=np.ascontiguousarray(L0_ffn_w_up, np.float32),
        wd=np.ascontiguousarray(L0_ffn_w_down, np.float32),
        bz=np.ascontiguousarray(L0_attn_bz, np.float32).reshape(1, DM),
        bh=np.ascontiguousarray(L0_attn_bh, np.float32).reshape(1, DM),
        gam=np.ascontiguousarray(
            np.broadcast_to(np.asarray(L0_attn_gamma, np.float32), (128, DM))),
        bet=np.ascontiguousarray(
            np.broadcast_to(np.asarray(L0_attn_beta, np.float32), (128, DM))),
        idb=np.eye(128, dtype=bf),
        idf=np.eye(128, dtype=np.float32),
    )
    in_maps = []
    xs = np.asarray(L0_attn_x, np.float32)
    for b in range(B):
        m = dict(common)
        m["x"] = np.ascontiguousarray(xs[b])
        in_maps.append(m)
    res = run_bass_kernel_spmd(nc, in_maps, core_ids=list(range(B)))
    out = np.stack([res.results[b]["out"] for b in range(B)], axis=0)
    return out


# version 14 (final)
# speedup vs baseline: 1.6660x; 1.6660x over previous
import contextlib
import numpy as np
import ml_dtypes
import concourse.bass as bass
import concourse.mybir as mybir
from concourse.bass_utils import run_bass_kernel_spmd

F32 = mybir.dt.float32
BF16 = mybir.dt.bfloat16
I32 = mybir.dt.int32
AF = mybir.ActivationFunctionType
ALU = mybir.AluOpType
AXX = mybir.AxisListType.X

T, DM, II = 4096, 1024, 2816
TS = 128
KC_D = DM // 128
KC_I = II // 128
NCH = 6
MA = float(1.5 * 2**23 * 0.5)
MB = float(1.5 * 2**23 * 1.0)
MA1 = 1.0 - MA
MBN = -(MB + 2.0)
MAGIC = float(0x5F3759E0)
EPS = 1e-5
INV_D = float(np.float32(1.0) / np.float32(DM))
INV6 = float(np.float32(1.0) / np.float32(6.0))
NEWTON = 2


class Sched:
    ENGS = ("PE", "DV", "AC", "GP", "SQ")

    def __init__(self):
        self.lists = {}
        self.cur = None
        self.ops = []

    def use(self, name):
        self.lists.setdefault(name, [])
        self.cur = self.lists[name]

    def add(self, eng, fn, reads=(), writes=(), dma=None, tag="", hz=False,
            dma_final=False):
        self.cur.append(dict(eng=eng, fn=fn, reads=tuple(reads),
                             writes=tuple(writes), dma=dma, tag=tag, hz=hz,
                             dma_final=dma_final))

    @staticmethod
    def rr(a, b, skew=0):
        out = list(a[:skew])
        ia, ib = skew, 0
        while ia < len(a) or ib < len(b):
            if ia < len(a):
                out.append(a[ia])
                ia += 1
            if ib < len(b):
                out.append(b[ib])
                ib += 1
        return out

    def resolve(self):
        last_w, readers = {}, {}
        for i, op in enumerate(self.ops):
            deps = set()
            for r in op["reads"]:
                w = last_w.get(r)
                if w is not None:
                    deps.add(w)
            for w_ in op["writes"]:
                w = last_w.get(w_)
                if w is not None:
                    deps.add(w)
                for rd in readers.get(w_, ()):
                    deps.add(rd)
            deps.discard(i)
            op["deps"] = deps
            for r in op["reads"]:
                readers.setdefault(r, []).append(i)
            for w_ in op["writes"]:
                last_w[w_] = i
                readers[w_] = []
        marked = set()
        for op in self.ops:
            for d in op["deps"]:
                dop = self.ops[d]
                if (dop["dma"] or dop["eng"] != op["eng"] or dop["hz"]
                        or op["hz"]):
                    marked.add(d)
        cnt = {e: 0 for e in self.ENGS}
        qcnt = {}
        val = {}
        for i, op in enumerate(self.ops):
            if op["dma"]:
                qcnt[op["dma"]] = qcnt.get(op["dma"], 0) + 16
                val[i] = (op["dma"], qcnt[op["dma"]])
                op["inc"] = (op["dma"], 16)
            elif i in marked:
                cnt[op["eng"]] += 1
                val[i] = ("s" + op["eng"], cnt[op["eng"]])
                op["inc"] = ("s" + op["eng"], 1)
            else:
                op["inc"] = None
        val_true = dict(val)
        for i, op in enumerate(self.ops):
            if op["dma"] and op.get("dma_final"):
                val[i] = (op["dma"], qcnt[op["dma"]])
        watermark = {}
        for i, op in enumerate(self.ops):
            waits = {}
            for d in sorted(op["deps"]):
                dop = self.ops[d]
                if d not in val:
                    continue
                if (dop["eng"] == op["eng"] and not dop["dma"]
                        and not dop["hz"] and not op["hz"]):
                    continue
                sem, v = val[d]
                if (dop["dma"] and dop.get("dma_final")
                        and op["dma"] == dop["dma"]):
                    sem, v = val_true[d]
                key = (op["eng"], sem)
                if watermark.get(key, -1) >= v:
                    continue
                waits[sem] = max(waits.get(sem, 0), v)
            for sem, v in waits.items():
                watermark[(op["eng"], sem)] = v
            op["waits"] = sorted(waits.items())
        self.final_q = dict(qcnt)
        chk = {}
        for i, op in enumerate(self.ops):
            for sem, v in op["waits"]:
                assert v <= chk.get(sem, 0), (
                    f"wait-on-future: op {i} {op['tag']} {sem}>={v}")
            if op["inc"]:
                sem, n = op["inc"]
                chk[sem] = chk.get(sem, 0) + n

    def emit(self, sems):
        def run(stream, engobj):
            for op in self.ops:
                if op["eng"] != stream:
                    continue
                for sem, v in op["waits"]:
                    engobj.wait_ge(sems[sem], v)
                inst = op["fn"](engobj)
                if op["inc"] is not None:
                    sem, n = op["inc"]
                    inst.then_inc(sems[sem], n)
        return run


def build_nc(nstep=T // TS, dbg=frozenset(("steps", "gu", "wd", "wzwh"))):
    assert nstep % 2 == 0
    nc = bass.Bass("TRN2", target_bir_lowering=False)

    x_h = nc.dram_tensor("x", [T, DM], F32, kind="ExternalInput").ap()
    out_h = nc.dram_tensor("out", [T, DM], F32, kind="ExternalOutput").ap()
    wz_h = nc.dram_tensor("wz", [DM, DM], F32, kind="ExternalInput").ap()
    wh_h = nc.dram_tensor("wh", [DM, DM], F32, kind="ExternalInput").ap()
    wg_h = nc.dram_tensor("wg", [II, DM], F32, kind="ExternalInput").ap()
    wu_h = nc.dram_tensor("wu", [II, DM], F32, kind="ExternalInput").ap()
    wd_h = nc.dram_tensor("wd", [DM, II], F32, kind="ExternalInput").ap()
    bzt_h = nc.dram_tensor("bzt", [1, DM], F32, kind="ExternalInput").ap()
    bht_h = nc.dram_tensor("bht", [1, DM], F32, kind="ExternalInput").ap()
    gam_h = nc.dram_tensor("gam", [128, DM], F32, kind="ExternalInput").ap()
    bet_h = nc.dram_tensor("bet", [128, DM], F32, kind="ExternalInput").ap()
    idb_h = nc.dram_tensor("idb", [128, 128], BF16, kind="ExternalInput").ap()
    idf_h = nc.dram_tensor("idf", [128, 128], F32, kind="ExternalInput").ap()
    wdq_h = nc.dram_tensor("wdq", [II, DM], BF16, kind="Internal").ap()

    ctx = contextlib.ExitStack()
    sb = lambda n, s, d=F32: ctx.enter_context(nc.sbuf_tensor(n, s, d))
    ps = lambda n, s, d=F32: ctx.enter_context(nc.psum_tensor(n, s, d))

    WG = sb("WG", [128, KC_D * II], BF16)
    WU = sb("WU", [128, KC_D * II], BF16)
    WZR = sb("WZR", [128, KC_D * DM], BF16)
    WHR = sb("WHR", [128, KC_D * DM], BF16)
    GAMB = sb("GAMB", [128, DM], BF16)
    BETB = sb("BETB", [128, DM], BF16)
    IDB = sb("IDB", [128, 128], BF16)
    IDF = sb("IDF", [128, 128], F32)
    CARRY = sb("CARRY", [128, KC_D])
    EPST = sb("EPST", [128, 1])
    KON = sb("KON", [128, 3])
    ONESB = sb("ONESB", [1, 128], BF16)
    BZR = sb("BZR", [1, DM], BF16)
    BHR = sb("BHR", [1, DM], BF16)

    SX = [sb(f"SX{g}", [128, DM]) for g in range(2)]
    U = [sb(f"U{g}", [128, DM]) for g in range(2)]
    W = [sb(f"W{g}", [128, DM]) for g in range(2)]
    XM = [sb(f"XM{g}", [128, DM]) for g in range(2)]
    T1 = [sb(f"T1{g}", [128, 512]) for g in range(2)]
    T2 = [sb(f"T2{g}", [128, 512]) for g in range(2)]
    T3 = [sb(f"T3{g}", [128, 512]) for g in range(2)]
    T4 = [sb(f"T4{g}", [128, 512]) for g in range(2)]
    M1 = [sb(f"M1{g}", [128, 512]) for g in range(2)]
    QXB = [sb(f"QXB{g}", [128, DM], BF16) for g in range(2)]
    SGN = [sb(f"SGN{g}", [128, 512], BF16) for g in range(2)]
    XF = [sb(f"XF{g}", [128, DM], BF16) for g in range(2)]
    HQF = [[sb(f"HQF{g}_{i}", [128, 512], BF16) for i in range(2)]
           for g in range(2)]
    WDB = [[sb(f"WDB{g}_{i}", [128, DM], BF16) for i in range(2)]
           for g in range(2)]
    ST = [sb(f"STg{g}", [128, 24]) for g in range(2)]
    STI = [sb(f"STIg{g}", [128, 1], I32) for g in range(2)]
    STB = [sb(f"STBg{g}", [128, 64]) for g in range(2)]
    STB2 = [sb(f"STB2g{g}", [128, 64]) for g in range(2)]
    STR = [sb(f"STRg{g}", [128, 64]) for g in range(2)]
    STR6 = [sb(f"STR6g{g}", [128, 64]) for g in range(2)]
    STS = [sb(f"STSg{g}", [128, 64]) for g in range(2)]

    PA0 = [ps(f"PA0{g}", [128, 512]) for g in range(2)]
    PA1 = [ps(f"PA1{g}", [128, 512]) for g in range(2)]
    PB = [ps(f"PB{g}", [128, 512]) for g in range(2)]
    PT = [ps(f"PT{g}", [128, 512]) for g in range(2)]
    PTB = [PT[g][:].bitcast(BF16) for g in range(2)]

    STNAMES = ["sum", "ssq", "negm", "m", "va", "r1", "halfr1",
               "sdf", "rstd", "nt", "nt2", "m2e", "su0", "su1",
               "sq0", "sq1", "y1"]
    STCOL = []
    for g in range(2):
        STCOL.append({nm: ST[g][:, i_:i_ + 1] for i_, nm in enumerate(STNAMES)})

    sch = Sched()
    carry_seq = {k: [] for k in range(KC_D)}

    def dma(q, dst, src, key_w, keys_r=(), tag="", sem=None, final=False):
        sch.add(q, lambda e, d=dst, s=src: e.dma_start(out=d, in_=s),
                reads=keys_r,
                writes=(key_w,) if isinstance(key_w, str) else key_w,
                dma=sem, tag=tag, dma_final=final)

    def rsqrt_chain(g, prefix):
        C = STCOL[g]
        va, sdf, rstd, nt = C["va"], C["sdf"], C["rstd"], C["nt"]
        nt2, y1 = C["nt2"], C["y1"]
        stk = f"ST{g}"
        sch.add("DV", lambda v, a=va, o=sdf: v.tensor_scalar(
            o, a.bitcast(I32), -0.5, MAGIC, ALU.mult, ALU.add),
            reads=[stk], writes=[stk], tag=prefix + "seed", hz=True)
        sch.add("AC", lambda s, i=sdf, g_=g: s.copy(STI[g_][:], i),
                reads=[stk], writes=[f"STI{g}"], tag=prefix + "f2i")
        sch.add("DV", lambda v, g_=g, o=rstd: v.tensor_copy(
            o, STI[g_][:].bitcast(F32)),
            reads=[f"STI{g}"], writes=[stk], tag=prefix + "i2f", hz=True)
        ys = [rstd, y1]
        assert NEWTON % 2 == 0
        for it in range(NEWTON):
            src, dst_ = ys[it % 2], ys[(it + 1) % 2]
            sch.add("DV", lambda v, a=src: v.tensor_tensor(nt, a, a, ALU.mult),
                    reads=[stk], writes=[stk], hz=True)
            sch.add("DV", lambda v: v.tensor_tensor(nt2, nt, va, ALU.mult),
                    reads=[stk], writes=[stk], hz=True)
            sch.add("DV", lambda v: v.tensor_scalar(nt, nt2, -0.5, 1.5,
                                                    ALU.mult, ALU.add),
                    reads=[stk], writes=[stk], hz=True)
            sch.add("DV", lambda v, a=src, o=dst_: v.tensor_tensor(
                o, a, nt, ALU.mult), reads=[stk], writes=[stk], hz=True)

    def ln_stats(g, src, src_key, prefix, need_m=False, need_r1=False,
                 sum_ready=False):
        C = STCOL[g]
        stk = f"ST{g}"
        if not sum_ready:
            sch.add("DV", lambda v, s=src: v.tensor_reduce(
                C["sum"], s, AXX, ALU.add),
                reads=[src_key], writes=[stk], tag=prefix + "sum", hz=True)
        sch.add("AC", lambda s, i=src: s.activation(
            T1[g][:], i[:, 0:512], AF.Square, accum_out=C["sq0"]),
            reads=[src_key], writes=[f"T1{g}", stk], tag=prefix + "ssq0")
        sch.add("AC", lambda s, i=src: s.activation(
            T2[g][:], i[:, 512:1024], AF.Square, accum_out=C["sq1"]),
            reads=[src_key], writes=[f"T2{g}", stk], tag=prefix + "ssq1")
        sch.add("DV", lambda v: v.tensor_tensor(
            C["ssq"], C["sq0"], C["sq1"], ALU.add),
            reads=[stk], writes=[stk], tag=prefix + "ssqsum", hz=True)
        sch.add("DV", lambda v: v.tensor_scalar(
            C["negm"], C["sum"], -INV_D, None, ALU.mult),
            reads=[stk], writes=[stk], hz=True)
        if need_m:
            sch.add("DV", lambda v: v.tensor_scalar(
                C["m"], C["sum"], INV_D, None, ALU.mult),
                reads=[stk], writes=[stk], hz=True)
        sch.add("DV", lambda v: v.scalar_tensor_tensor(
            C["m2e"], C["negm"], C["negm"], EPST[:],
            ALU.mult, ALU.subtract),
            reads=[stk, "EPST"], writes=[stk], hz=True)
        sch.add("DV", lambda v: v.scalar_tensor_tensor(
            C["va"], C["ssq"], INV_D, C["m2e"],
            ALU.mult, ALU.subtract),
            reads=[stk], writes=[stk], hz=True)
        if need_r1:
            sch.add("DV", lambda v: v.reciprocal(C["r1"], C["va"]),
                    reads=[stk], writes=[stk], hz=True)
            sch.add("DV", lambda v: v.tensor_scalar(
                C["halfr1"], C["r1"], 0.5, None, ALU.mult),
                reads=[stk], writes=[stk], hz=True)
        rsqrt_chain(g, prefix)

    def quant(g, src, src_key, dst, dst_key, w, tmps, sgn_view, sgn_key,
              nblk_off, use_negm, prefix):
        C = STCOL[g]
        stk = f"ST{g}"
        (t1, k1), (t2, k2), (t3, k3), (t4, k4) = tmps
        nb = w // 16
        bmv = STB[g][:, nblk_off:nblk_off + nb]
        bm2 = STB2[g][:, nblk_off:nblk_off + nb]
        rrv = STR[g][:, nblk_off:nblk_off + nb]
        rr6 = STR6[g][:, nblk_off:nblk_off + nb]
        ssv = STS[g][:, nblk_off:nblk_off + nb]
        negm = C["negm"]
        if use_negm:
            sch.add("AC", lambda s, i=src, o=t1: s.activation(
                o, i, AF.Abs, bias=negm), reads=[src_key, stk],
                writes=[k1], tag=prefix + "abs")
            sch.add("AC", lambda s, i=src, o=sgn_view: s.activation(
                o, i, AF.Sign, bias=negm), reads=[src_key, stk],
                writes=[sgn_key])
        else:
            sch.add("AC", lambda s, i=src, o=t1: s.activation(o, i, AF.Abs),
                    reads=[src_key], writes=[k1], tag=prefix + "abs")
            sch.add("AC", lambda s, i=src, o=sgn_view: s.activation(
                o, i, AF.Sign), reads=[src_key], writes=[sgn_key])
        sch.add("DV", lambda v, i=t1, o=bmv: v.tensor_reduce(
            o, i.rearrange("p (b i) -> p b i", i=16), AXX, ALU.max),
            reads=[k1], writes=[f"STB{g}"], tag=prefix + "bmax", hz=True)
        sch.add("DV", lambda v, i=bmv, o=bm2: v.tensor_scalar(
            o, i, 6e-12, None, ALU.max), reads=[f"STB{g}"],
            writes=[f"STB2{g}"], hz=True)
        sch.add("DV", lambda v, i=bm2, o=rrv: v.reciprocal(o, i),
                reads=[f"STB2{g}"], writes=[f"STR{g}"], hz=True)
        sch.add("DV", lambda v, i=rrv, o=rr6: v.tensor_scalar(
            o, i, 6.0, None, ALU.mult), reads=[f"STR{g}"],
            writes=[f"STR6{g}"], hz=True)
        sch.add("DV", lambda v, i=bm2, o=ssv: v.tensor_scalar(
            o, i, INV6, None, ALU.mult), reads=[f"STB2{g}"],
            writes=[f"STS{g}"], hz=True)
        sch.add("DV", lambda v, i=t1, o=t2, r=rr6, nb_=nb: v.tensor_tensor(
            o.rearrange("p (b i) -> p b i", i=16),
            i.rearrange("p (b i) -> p b i", i=16),
            r.unsqueeze(2).broadcast_to([128, nb_, 16]), ALU.mult),
            reads=[k1, f"STR6{g}"], writes=[k2], tag=prefix + "xn")
        sch.add("AC", lambda s, i=t2, o=t3: s.activation(
            o, i, AF.Abs, bias=KON[:, 0:1]), reads=[k2, "KON"], writes=[k3],
            tag=prefix + "rA")
        sch.add("DV", lambda v, i=t3, o=t4: v.tensor_scalar(
            o, i, MA1, 3.0, ALU.add, ALU.min), reads=[k3], writes=[k4])
        sch.add("AC", lambda s, i=t2, o=t1: s.activation(
            o, i, AF.Abs, bias=KON[:, 1:2]), reads=[k2, "KON"], writes=[k1],
            tag=prefix + "rB")
        sch.add("DV", lambda v, i=t1, o=t3: v.tensor_scalar(
            o, i, MBN, 0.0, ALU.add, ALU.max), reads=[k1], writes=[k3])
        sch.add("AC", lambda s, i=t2, o=t1: s.activation(
            o, i, AF.Sign, bias=KON[:, 2:3]), reads=[k2, "KON"], writes=[k1],
            tag=prefix + "s5")
        sch.add("DV", lambda v, a=t3, b=t4, o=t2: v.scalar_tensor_tensor(
            o, a, 2.0, b, ALU.min, ALU.add), reads=[k3, k4], writes=[k2])
        sch.add("DV", lambda v, a=t2, b=t1, o=t4: v.tensor_tensor(
            o, a, b, ALU.add), reads=[k2, k1], writes=[k4])
        sch.add("DV", lambda v, i=t4, o=t2, s_=ssv, nb_=nb: v.tensor_tensor(
            o.rearrange("p (b i) -> p b i", i=16),
            i.rearrange("p (b i) -> p b i", i=16),
            s_.unsqueeze(2).broadcast_to([128, nb_, 16]), ALU.mult),
            reads=[k4, f"STS{g}"], writes=[k2])
        sch.add("DV", lambda v, i=t2, o=dst, sg=sgn_view: v.tensor_tensor(
            o, i, sg, ALU.mult), reads=[k2, sgn_key], writes=[dst_key],
            tag=prefix + "deq")

    trh = [0, 0]

    def tr_group(g, srcs, src_keys, dst, dst_key, dtype, prefix,
                 skip_evac=False, skip_tr=False):
        n = len(srcs)
        if dtype == "bf":
            assert n <= 4
            half = trh[g]
            base = half * 4
            pv, ident, idk = PTB[g], IDB, "IDENT"
        else:
            assert n <= 2
            half = trh[g]
            base = half * 2
            pv, ident, idk = PT[g][:], IDF, "IDENT2"
        pk = f"PT{g}"
        trh[g] ^= 1
        for i, (s_, skl) in enumerate(
                [] if skip_tr else list(zip(srcs, src_keys))):
            sch.add("PE", lambda t, s2=s_, sl=base + i, pv_=pv, id_=ident:
                    t.transpose(pv_[:, sl * 128:(sl + 1) * 128], s2, id_[:]),
                    reads=list(skl) + [idk], writes=[pk],
                    tag=f"{prefix}tr{i}")
        if skip_evac:
            return
        def _ev(s, pv_=pv, b=base, n_=n, d=dst):
            s2 = pv_[:, b * 128:(b + n_) * 128]
            if len(d.shape) == 3:
                s2 = s2.rearrange("p (k n) -> p k n", n=d.shape[2])
            return s.copy(d, s2)
        sch.add("AC", _ev,
                reads=[pk] if not skip_tr else [], writes=[dst_key]
                if isinstance(dst_key, str) else list(dst_key),
                tag=f"{prefix}ev")

    def lnq(g, src, src_key, pfx, sum_ready, lvl=5):
        ln_stats(g, src, src_key, pfx + "ln_", need_r1=True,
                 sum_ready=sum_ready)
        if lvl < 2:
            return
        tmps = [(T1[g][:], f"T1{g}"), (T2[g][:], f"T2{g}"),
                (T3[g][:], f"T3{g}"), (T4[g][:], f"T4{g}")]
        for hh in range(2):
            hs = hh * 512
            quant(g, src[:, hs:hs + 512], src_key, QXB[g][:, hs:hs + 512],
                  f"QXB{g}", 512, tmps, SGN[g][:, 0:512], f"SGN{g}",
                  32 * hh, use_negm=True, prefix=pfx + f"q{hh}_")
        if lvl < 3:
            return
        for grp in range(2):
            srcs = [QXB[g][:, (grp * 4 + i) * 128:(grp * 4 + i + 1) * 128]
                    for i in range(4)]
            tr_group(g, srcs, [[f"QXB{g}"]] * 4,
                     XF[g][:, grp * 512:(grp + 1) * 512], f"XF{g}",
                     "bf", pfx + f"t{grp}_", skip_evac=(lvl == 3),
                     skip_tr=(lvl == 4))

    def ffn(g, res, res_key, dst, dst_key, pfx):
        C = STCOL[g]
        stk = f"ST{g}"
        dq = "SQ"

        def wld(kg):
            if kg >= KC_I:
                return
            dma(dq, WDB[g][kg % 2][:], wdq_h[kg * 128:(kg + 1) * 128, :],
                f"WDB{g}_{kg % 2}", keys_r=[f"WDQ{kg}"],
                tag=f"{pfx}ld{kg}", sem=f"qD{g}{kg % 2}")

        wld(0)
        wld(1)
        qtmps = [(T1[g][:], f"T1{g}"), (T2[g][:], f"T2{g}"),
                 (T3[g][:], f"T3{g}"), (M1[g][:], f"M1{g}")]
        for c in range(NCH):
            lo = c * 512
            w = min(512, II - lo)
            nkk = w // 128
            wchs = [f"WU{ch}" for ch in range(lo // 128, (lo + w) // 128)]
            gchs = [f"WG{ch}" for ch in range(lo // 128, (lo + w) // 128)]
            for kc in range(KC_D):
                sch.add("PE", lambda t, kc_=kc, lo_=lo, w_=w: t.matmul(
                    PB[g][:, 0:w_], XF[g][:, kc_ * 128:(kc_ + 1) * 128],
                    WU[:, kc_ * II + lo_:kc_ * II + lo_ + w_],
                    start=(kc_ == 0), stop=(kc_ == KC_D - 1)),
                    reads=[f"XF{g}"] + wchs, writes=[f"PB{g}"],
                    tag=f"{pfx}u{c}")
            sch.add("AC", lambda s, w_=w: s.copy(T1[g][:, 0:w_],
                                                 PB[g][:, 0:w_]),
                    reads=[f"PB{g}"], writes=[f"T1{g}"], tag=f"{pfx}ur{c}")
            for kc in range(KC_D):
                sch.add("PE", lambda t, kc_=kc, lo_=lo, w_=w: t.matmul(
                    PB[g][:, 0:w_], XF[g][:, kc_ * 128:(kc_ + 1) * 128],
                    WG[:, kc_ * II + lo_:kc_ * II + lo_ + w_],
                    start=(kc_ == 0), stop=(kc_ == KC_D - 1)),
                    reads=[f"XF{g}"] + gchs, writes=[f"PB{g}"],
                    tag=f"{pfx}g{c}")
            sch.add("AC", lambda s, w_=w: s.activation(
                T2[g][:, 0:w_], PB[g][:, 0:w_], AF.Sigmoid, scale=C["rstd"]),
                reads=[f"PB{g}", stk], writes=[f"T2{g}"], tag=f"{pfx}sg{c}")
            sch.add("DV", lambda v, w_=w: v.tensor_tensor(
                T3[g][:, 0:w_], T2[g][:, 0:w_], PB[g][:, 0:w_], ALU.mult),
                reads=[f"T2{g}", f"PB{g}"], writes=[f"T3{g}"],
                tag=f"{pfx}th{c}")
            sch.add("DV", lambda v, w_=w: v.tensor_tensor(
                T4[g][:, 0:w_], T3[g][:, 0:w_], T1[g][:, 0:w_], ALU.mult),
                reads=[f"T3{g}", f"T1{g}"], writes=[f"T4{g}"],
                tag=f"{pfx}h{c}")
            quant(g, T4[g][:, 0:w], f"T4{g}", QXB[g][:, 0:w], f"QXB{g}", w,
                  [(t_[:, 0:w], k_) for t_, k_ in qtmps],
                  SGN[g][:, 0:w], f"SGN{g}", 0,
                  use_negm=False, prefix=f"{pfx}qh{c}_")
            hf = HQF[g][c % 2]
            hfk = f"HQF{g}_{c % 2}"
            tr_group(g, [QXB[g][:, kk * 128:(kk + 1) * 128]
                         for kk in range(nkk)],
                     [[f"QXB{g}"]] * nkk, hf[:, 0:nkk * 128], hfk,
                     "bf", f"{pfx}ht{c}_")
            for kk in range(nkk):
                kg = 4 * c + kk
                wb = WDB[g][kg % 2]
                wbk = f"WDB{g}_{kg % 2}"
                sch.add("PE", lambda t, kk_=kk, kg_=kg, wb_=wb, hf_=hf:
                        t.matmul(
                            PA0[g][:], hf_[:, kk_ * 128:(kk_ + 1) * 128],
                            wb_[:, 0:512], start=(kg_ == 0),
                            stop=(kg_ == KC_I - 1)),
                        reads=[hfk, wbk], writes=[f"PA0{g}"],
                        tag=f"{pfx}d0_{kg}")
                sch.add("PE", lambda t, kk_=kk, kg_=kg, wb_=wb, hf_=hf:
                        t.matmul(
                            PA1[g][:], hf_[:, kk_ * 128:(kk_ + 1) * 128],
                            wb_[:, 512:1024], start=(kg_ == 0),
                            stop=(kg_ == KC_I - 1)),
                        reads=[hfk, wbk], writes=[f"PA1{g}"],
                        tag=f"{pfx}d1_{kg}")
                wld(kg + 2)
        sch.add("DV", lambda v: v.scalar_tensor_tensor(
            dst[:, 0:512], PA0[g][:], C["halfr1"], res[:, 0:512],
            ALU.mult, ALU.add, accum_out=C["su0"]),
            reads=[f"PA0{g}", stk, res_key], writes=[dst_key, stk],
            tag=f"{pfx}ev0", hz=True)
        sch.add("DV", lambda v: v.scalar_tensor_tensor(
            dst[:, 512:1024], PA1[g][:], C["halfr1"], res[:, 512:1024],
            ALU.mult, ALU.add, accum_out=C["su1"]),
            reads=[f"PA1{g}", stk, res_key], writes=[dst_key, stk],
            tag=f"{pfx}ev1", hz=True)
        sch.add("DV", lambda v: v.tensor_tensor(
            C["sum"], C["su0"], C["su1"], ALU.add),
            reads=[stk], writes=[stk], tag=f"{pfx}sumu", hz=True)

    def stageCD(j, lvl=4):
        g = j % 2
        C = STCOL[g]
        stk = f"ST{g}"
        pfx = f"s{j}_"
        ln_stats(g, U[g][:], f"U{g}", pfx + "ln2_", need_m=True,
                 sum_ready=(lvl >= 40))
        if lvl < 12:
            return
        sch.add("DV", lambda v: v.tensor_scalar(
            XM[g][:], U[g][:], C["m"], C["rstd"], ALU.subtract, ALU.mult),
            reads=[f"U{g}", stk], writes=[f"XM{g}"], tag=pfx + "xm")
        if lvl < 13:
            return
        sch.add("AC", lambda s: s.copy(QXB[g][:], XM[g][:]),
                reads=[f"XM{g}"], writes=[f"QXB{g}"], tag=pfx + "xmb")
        if lvl < 14:
            return
        for grp in range(2):
            srcs = [QXB[g][:, (grp * 4 + i) * 128:(grp * 4 + i + 1) * 128]
                    for i in range(4)]
            tr_group(g, srcs, [[f"QXB{g}"]] * 4,
                     XF[g][:, grp * 512:(grp + 1) * 512], f"XF{g}",
                     "bf", pfx + f"xt{grp}_", skip_evac=(lvl == 14),
                     skip_tr=(lvl == 15))
        if lvl < 20:
            return
        for half in range(2):
            hs = half * 512
            nbias = (lvl != 24)
            for kd in range(KC_D):
                sch.add("PE", lambda t, kd_=kd, hs_=hs, nb_=nbias: t.matmul(
                    PA0[g][:], XF[g][:, kd_ * 128:(kd_ + 1) * 128],
                    WZR[:, kd_ * DM + hs_:kd_ * DM + hs_ + 512],
                    start=(kd_ == 0), stop=(not nb_ and kd_ == KC_D - 1)),
                    reads=[f"XF{g}", f"WZR{kd}"], writes=[f"PA0{g}"],
                    tag=pfx + f"mk{half}_{kd}")
            if nbias:
                sch.add("PE", lambda t, hs_=hs: t.matmul(
                    PA0[g][:], ONESB[:, 0:128], BZR[:, hs_:hs_ + 512],
                    start=False, stop=True),
                    reads=["ONESB", "BZR"], writes=[f"PA0{g}"],
                    tag=pfx + f"mkb{half}")
            for kd in range(KC_D):
                sch.add("PE", lambda t, kd_=kd, hs_=hs, nb_=nbias: t.matmul(
                    PA1[g][:], XF[g][:, kd_ * 128:(kd_ + 1) * 128],
                    WHR[:, kd_ * DM + hs_:kd_ * DM + hs_ + 512],
                    start=(kd_ == 0), stop=(not nb_ and kd_ == KC_D - 1)),
                    reads=[f"XF{g}", f"WHR{kd}"], writes=[f"PA1{g}"],
                    tag=pfx + f"ma{half}_{kd}")
            if nbias:
                sch.add("PE", lambda t, hs_=hs: t.matmul(
                    PA1[g][:], ONESB[:, 0:128], BHR[:, hs_:hs_ + 512],
                    start=False, stop=True),
                    reads=["ONESB", "BHR"], writes=[f"PA1{g}"],
                    tag=pfx + f"mab{half}")
            if lvl in (21, 24):
                continue
            sch.add("AC", lambda s, hs_=hs: s.activation(
                SX[g][:, hs_:hs_ + 512], PA0[g][:], AF.Sigmoid, scale=-1.0),
                reads=[f"PA0{g}"], writes=[f"SX{g}"], tag=pfx + f"c{half}")
            sch.add("AC", lambda s: s.activation(
                T1[g][:], PA0[g][:], AF.Sigmoid),
                reads=[f"PA0{g}"], writes=[f"T1{g}"], tag=pfx + f"z{half}")
            sch.add("AC", lambda s: s.activation(
                T2[g][:], PA1[g][:], AF.Sigmoid),
                reads=[f"PA1{g}"], writes=[f"T2{g}"], tag=pfx + f"sa{half}")
            sch.add("DV", lambda v: v.scalar_tensor_tensor(
                T3[g][:], PA1[g][:], 0.5, T2[g][:], ALU.add, ALU.max),
                reads=[f"PA1{g}", f"T2{g}"], writes=[f"T3{g}"],
                tag=pfx + f"g{half}")
            sch.add("DV", lambda v, hs_=hs: v.tensor_tensor(
                W[g][:, hs_:hs_ + 512], T1[g][:], T3[g][:], ALU.mult),
                reads=[f"T1{g}", f"T3{g}"], writes=[f"W{g}"],
                tag=pfx + f"v{half}")
        if lvl < 30:
            return
        for kc in range(KC_D):
            tq = (kc % 2) * 256
            tr_group(g, [SX[g][:, kc * 128:(kc + 1) * 128],
                         W[g][:, kc * 128:(kc + 1) * 128]],
                     [[f"SX{g}"], [f"W{g}"]],
                     T4[g][:, tq:tq + 256], f"T4{g}", "f32",
                     pfx + f"cv{kc}_")
            sl = kc % 4
            sch.add("DV", lambda v, kc_=kc, sl_=sl, tq_=tq:
                    v.tensor_tensor_scan(
                        T3[g][:, sl_ * 128:(sl_ + 1) * 128],
                        T4[g][:, tq_:tq_ + 128],
                        T4[g][:, tq_ + 128:tq_ + 256],
                        CARRY[:, kc_:kc_ + 1], ALU.mult, ALU.add),
                    reads=[f"T4{g}", f"CAR{kc}"], writes=[f"T3{g}"],
                    tag=f"mscan{j}_{kc}", hz=True)
            carry_seq[kc].append(("scan", j))
            sch.add("DV", lambda v, kc_=kc, sl_=sl: v.tensor_copy(
                CARRY[:, kc_:kc_ + 1],
                T3[g][:, sl_ * 128 + 127:sl_ * 128 + 128]),
                reads=[f"T3{g}"], writes=[f"CAR{kc}"],
                tag=f"mccp{j}_{kc}", hz=True)
            carry_seq[kc].append(("ccp", j))
            tr_group(g, [T3[g][:, sl * 128:(sl + 1) * 128]],
                     [[f"T3{g}"]],
                     SX[g][:, kc * 128:(kc + 1) * 128], f"SX{g}",
                     "f32", pfx + f"hb{kc}_")
        if lvl < 40:
            return
        ln_stats(g, SX[g][:], f"SX{g}", pfx + "lnh_", need_m=True)
        for hh in range(2):
            hs = hh * 512
            sch.add("DV", lambda v, hs_=hs: v.tensor_scalar(
                T1[g][:], SX[g][:, hs_:hs_ + 512], C["m"], C["rstd"],
                ALU.subtract, ALU.mult),
                reads=[f"SX{g}", stk], writes=[f"T1{g}"], tag=pfx + f"hn{hh}")
            sch.add("DV", lambda v, hs_=hs: v.tensor_tensor(
                T2[g][:], T1[g][:], GAMB[:, hs_:hs_ + 512], ALU.mult),
                reads=[f"T1{g}", "GAMB"], writes=[f"T2{g}"],
                tag=pfx + f"o1{hh}")
            sch.add("DV", lambda v, hs_=hs: v.tensor_tensor(
                T3[g][:], XM[g][:, hs_:hs_ + 512], BETB[:, hs_:hs_ + 512],
                ALU.add),
                reads=[f"XM{g}", "BETB"], writes=[f"T3{g}"],
                tag=pfx + f"b1{hh}")
            sch.add("GP", lambda p, hs_=hs: p.tensor_tensor(
                T4[g][:], T3[g][:], U[g][:, hs_:hs_ + 512], ALU.add),
                reads=[f"T3{g}", f"U{g}"], writes=[f"T4{g}"],
                tag=pfx + f"b2{hh}")
            sch.add("DV", lambda v, hs_=hs, su=C["su0" if hh == 0 else "su1"]:
                    v.scalar_tensor_tensor(
                        W[g][:, hs_:hs_ + 512], T2[g][:], 1.0, T4[g][:],
                        ALU.mult, ALU.add, accum_out=su),
                    reads=[f"T2{g}", f"T4{g}"], writes=[f"W{g}", stk],
                    tag=pfx + f"w{hh}", hz=True)
        sch.add("DV", lambda v: v.tensor_tensor(
            C["sum"], C["su0"], C["su1"], ALU.add),
            reads=[stk], writes=[stk], tag=pfx + "wsum", hz=True)

    def lane_steps(g):
        dq = "SQ"
        oq = "SQ"

        def prefetch_x(j):
            if j < nstep:
                dma(dq, SX[g][:], x_h[j * TS:(j + 1) * TS, :], f"SX{g}",
                    tag=f"s{j}_ldx", sem=f"qX{g}")

        prefetch_x(g)
        for j in range(g, nstep, 2):
            pfx = f"s{j}_"
            if "sA" in dbg or "steps" in dbg:
                lnq(g, SX[g][:], f"SX{g}", pfx + "A_", sum_ready=False)
            elif "sA1" in dbg:
                lnq(g, SX[g][:], f"SX{g}", pfx + "A_", sum_ready=False,
                    lvl=1)
            elif "sA2" in dbg:
                lnq(g, SX[g][:], f"SX{g}", pfx + "A_", sum_ready=False,
                    lvl=2)
            elif "sA3" in dbg:
                lnq(g, SX[g][:], f"SX{g}", pfx + "A_", sum_ready=False,
                    lvl=3)
            elif "sA4" in dbg:
                lnq(g, SX[g][:], f"SX{g}", pfx + "A_", sum_ready=False,
                    lvl=4)
            if "sB" in dbg or "steps" in dbg:
                ffn(g, SX[g][:], f"SX{g}", U[g][:], f"U{g}", pfx + "f1_")
            if "sCD" in dbg or "steps" in dbg:
                stageCD(j, lvl=40)
            elif "sC1" in dbg:
                stageCD(j, lvl=16)
            elif "sC11" in dbg:
                stageCD(j, lvl=11)
            elif "sC12" in dbg:
                stageCD(j, lvl=12)
            elif "sC13" in dbg:
                stageCD(j, lvl=13)
            elif "sC14" in dbg:
                stageCD(j, lvl=14)
            elif "sC15" in dbg:
                stageCD(j, lvl=15)
            elif "sC2" in dbg:
                stageCD(j, lvl=20)
            elif "sC21" in dbg:
                stageCD(j, lvl=21)
            elif "sC24" in dbg:
                stageCD(j, lvl=24)
            elif "sC3" in dbg:
                stageCD(j, lvl=30)
            if "sE" in dbg or "steps" in dbg:
                lnq(g, W[g][:], f"W{g}", pfx + "E_", sum_ready=True)
                ffn(g, W[g][:], f"W{g}", SX[g][:], f"SX{g}", pfx + "f2_")
            dma(oq, out_h[j * TS:(j + 1) * TS, :], SX[g][:], f"OUT{j}",
                keys_r=[f"SX{g}"], tag=f"s{j}_stout", sem=f"qO{g}")
            prefetch_x(j + 2)

    def prep_lane(g):
        dq = "SQ"
        tmps = [(T1[g][:], f"T1{g}"), (T2[g][:], f"T2{g}"),
                (T3[g][:], f"T3{g}"), (T4[g][:], f"T4{g}")]

        def quant_wide(src, pw, pfx):
            widths = []
            off = 0
            while off < pw:
                widths.append((off, min(512, pw - off)))
                off += 512
            for hh, (hs, w_) in enumerate(widths):
                quant(g, src[:, hs:hs + w_], f"SX{g}",
                      QXB[g][:, hs:hs + w_], f"QXB{g}", w_,
                      [(t[0][:, 0:w_], t[1]) for t in tmps],
                      SGN[g][:, 0:w_], f"SGN{g}", 32 * hh,
                      use_negm=False, prefix=pfx + f"q{hh}_")

        if g == 0:
            dma(dq, IDB[:], idb_h, "IDENT", tag="ldidb", sem="qC0",
                final=True)
            dma(dq, IDF[:], idf_h, "IDENT2", tag="ldidf", sem="qC0",
                final=True)
            dma(dq, SX[g][0:1, :], bzt_h, f"SX{g}", tag="ldbz",
                sem="qPa0")
            sch.add("AC", lambda s: s.copy(BZR[:], SX[g][0:1, :]),
                    reads=[f"SX{g}"], writes=["BZR"], tag="bzr")
            sch.add("GP", lambda p: p.memset(ONESB[:], 1.0),
                    writes=["ONESB"])
            sch.add("GP", lambda p: p.memset(CARRY[:], 0.0),
                    writes=tuple(f"CAR{k}" for k in range(KC_D)))
            sch.add("GP", lambda p: p.memset(EPST[:], EPS), writes=["EPST"])
            sch.add("GP", lambda p: p.memset(KON[:, 0:1], MA), writes=["KON"])
            sch.add("GP", lambda p: p.memset(KON[:, 1:2], MB), writes=["KON"])
            sch.add("GP", lambda p: p.memset(KON[:, 2:3], -5.0),
                    writes=["KON"])
            dma(dq, SX[g][:], gam_h, f"SX{g}", tag="ldgam", sem="qPa0")
            sch.add("AC", lambda s: s.copy(GAMB[:], SX[g][:]),
                    reads=[f"SX{g}"], writes=["GAMB"], tag="gamb")
            for kd in range(KC_D if "wzwh" in dbg else 0):
                dma(dq, U[g][:], wz_h[kd * 128:(kd + 1) * 128, :],
                    f"U{g}", tag=f"ldwz{kd}", sem="qPb0")
                sch.add("AC", lambda s, kd_=kd: s.copy(
                    WZR[:, kd_ * DM:(kd_ + 1) * DM], U[g][:]),
                    reads=[f"U{g}"], writes=[f"WZR{kd}"],
                    tag=f"wzb{kd}")
        else:
            dma(dq, SX[g][0:1, :], bht_h, f"SX{g}", tag="ldbh",
                sem="qPa1")
            sch.add("AC", lambda s: s.copy(BHR[:], SX[g][0:1, :]),
                    reads=[f"SX{g}"], writes=["BHR"], tag="bhr")
            dma(dq, SX[g][:], bet_h, f"SX{g}", tag="ldbet", sem="qPa1")
            sch.add("AC", lambda s: s.copy(BETB[:], SX[g][:]),
                    reads=[f"SX{g}"], writes=["BETB"], tag="betb")
            for kd in range(KC_D if "wzwh" in dbg else 0):
                dma(dq, U[g][:], wh_h[kd * 128:(kd + 1) * 128, :],
                    f"U{g}", tag=f"ldwh{kd}", sem="qPb1")
                sch.add("AC", lambda s, kd_=kd: s.copy(
                    WHR[:, kd_ * DM:(kd_ + 1) * DM], U[g][:]),
                    reads=[f"U{g}"], writes=[f"WHR{kd}"],
                    tag=f"whb{kd}")

        w_hbm, Wdst, wnm = (wg_h, WG, "WG") if g == 0 else (wu_h, WU, "WU")
        for ch in range(KC_I if "gu" in dbg else 0):
            dma(dq, SX[g][:], w_hbm[ch * 128:(ch + 1) * 128, :],
                f"SX{g}", tag=f"ld{wnm}{ch}", sem=f"qPa{g}")
            quant_wide(SX[g][:], DM, f"p{wnm}{ch}_")
            for grp in range(2):
                srcs = [QXB[g][:, (grp * 4 + i) * 128:
                               (grp * 4 + i + 1) * 128] for i in range(4)]
                dstv = Wdst[:, :].rearrange(
                    "p (kcd i) -> p kcd i", i=II)[
                        :, grp * 4:grp * 4 + 4, ch * 128:ch * 128 + 128]
                tr_group(g, srcs, [[f"QXB{g}"]] * 4, dstv, f"{wnm}{ch}",
                         "bf", f"p{wnm}t{ch}_{grp}_")

        for ch in range(g if "wd" in dbg else KC_D, KC_D, 2):
            for pi, (plo, pw) in enumerate([(0, 1024), (1024, 1024),
                                            (2048, 768)]):
                dma(dq, SX[g][:, 0:pw],
                    wd_h[ch * 128:(ch + 1) * 128, plo:plo + pw],
                    f"SX{g}", tag=f"ldwd{ch}_{pi}", sem=f"qPa{g}")
                quant_wide(SX[g][:, 0:pw], pw, f"pd{ch}_{pi}_")
                nk = pw // 128
                done = 0
                while done < nk:
                    n_ = min(4, nk - done)
                    kg0 = plo // 128 + done
                    srcs = [QXB[g][:, (done + i) * 128:(done + i + 1) * 128]
                            for i in range(n_)]
                    hf = HQF[g][0]
                    hfk = f"HQF{g}_0"
                    tr_group(g, srcs, [[f"QXB{g}"]] * n_,
                             hf[:, 0:n_ * 128], hfk, "bf",
                             f"pdt{ch}_{pi}_{done}_")
                    dstv = wdq_h[kg0 * 128:(kg0 + n_) * 128,
                                 ch * 128:(ch + 1) * 128].rearrange(
                        "(kk p) n -> p kk n", p=128)
                    srcv = hf[:, 0:n_ * 128].rearrange(
                        "p (kk n) -> p kk n", n=128)
                    dma(dq, dstv, srcv,
                        tuple(f"WDQ{kg0 + i}" for i in range(n_)),
                        keys_r=[hfk], tag=f"stwd{ch}_{pi}_{done}",
                        sem=f"qW{g}")
                    done += n_

    sch.use("prep0")
    prep_lane(0)
    sch.use("prep1")
    prep_lane(1)
    stepish = dbg & {"steps", "sA", "sA1", "sA2", "sA3", "sA4", "sB", "sCD", "sC1", "sC11", "sC12", "sC13", "sC14", "sC15", "sC2", "sC21", "sC24", "sC3", "sE", "sO"}
    sch.use("steps0")
    if stepish:
        lane_steps(0)
    sch.use("steps1")
    if stepish:
        lane_steps(1)

    skew = max(1, len(sch.lists["steps0"]) // (nstep // 2) // 2)
    if "seqmerge" in dbg:
        sch.ops = (Sched.rr(sch.lists["prep0"], sch.lists["prep1"])
                   + sch.lists["steps0"] + sch.lists["steps1"])
    else:
        sch.ops = (Sched.rr(sch.lists["prep0"], sch.lists["prep1"])
                   + Sched.rr(sch.lists["steps0"], sch.lists["steps1"],
                              skew=skew))

    pos = {}
    for i, op in enumerate(sch.ops):
        tg = op["tag"]
        if tg.startswith("mscan") or tg.startswith("mccp"):
            pos[tg] = i
    for kc in range(KC_D if ("steps" in dbg or "sCD" in dbg or "sC3" in dbg)
                    and "seqmerge" not in dbg else 0):
        seq = []
        for j in range(nstep):
            seq.append(f"mscan{j}_{kc}")
            seq.append(f"mccp{j}_{kc}")
        idx = [pos[t_] for t_ in seq]
        assert idx == sorted(idx), f"carry order violated at col {kc}"

    sch.resolve()

    sems = {}
    semnames = set(["sPE", "sDV", "sAC", "sGP", "sSQ"])
    for op in sch.ops:
        if op["dma"]:
            semnames.add(op["dma"])
    for s_ in sorted(semnames):
        sems[s_] = ctx.enter_context(nc.semaphore(s_))
    block = ctx.enter_context(nc.Block())
    run = sch.emit(sems)

    @block.tensor
    def _(tensor):
        run("PE", tensor)

    @block.vector
    def _(vector):
        run("DV", vector)

    @block.scalar
    def _(scalar):
        run("AC", scalar)

    @block.gpsimd
    def _(gpsimd):
        run("GP", gpsimd)
        for s_ in sorted(sch.final_q):
            gpsimd.wait_ge(sems[s_], sch.final_q[s_])

    @block.sync
    def _(sync):
        run("SQ", sync)

    ctx.close()
    return nc


_NC_CACHE = {}


def kernel(L0_attn_x, L0_attn_Wz, L0_attn_bz, L0_attn_Wh, L0_attn_bh,
           L0_attn_gamma, L0_attn_beta, L0_ffn_w_gate, L0_ffn_w_up,
           L0_ffn_w_down):
    B = L0_attn_x.shape[0]
    if "nc" not in _NC_CACHE:
        _NC_CACHE["nc"] = build_nc()
    nc = _NC_CACHE["nc"]
    bf = ml_dtypes.bfloat16
    common = dict(
        wz=np.ascontiguousarray(L0_attn_Wz, np.float32),
        wh=np.ascontiguousarray(L0_attn_Wh, np.float32),
        wg=np.ascontiguousarray(L0_ffn_w_gate, np.float32),
        wu=np.ascontiguousarray(L0_ffn_w_up, np.float32),
        wd=np.ascontiguousarray(L0_ffn_w_down, np.float32),
        bzt=np.ascontiguousarray(
            np.asarray(L0_attn_bz, np.float32).reshape(1, DM)),
        bht=np.ascontiguousarray(
            np.asarray(L0_attn_bh, np.float32).reshape(1, DM)),
        gam=np.ascontiguousarray(
            np.broadcast_to(np.asarray(L0_attn_gamma, np.float32),
                            (128, DM))),
        bet=np.ascontiguousarray(
            np.broadcast_to(np.asarray(L0_attn_beta, np.float32),
                            (128, DM))),
        idb=np.eye(128, dtype=bf),
        idf=np.eye(128, dtype=np.float32),
    )
    in_maps = []
    xs = np.asarray(L0_attn_x, np.float32)
    for b in range(B):
        m = dict(common)
        m["x"] = np.ascontiguousarray(xs[b])
        in_maps.append(m)
    res = run_bass_kernel_spmd(nc, in_maps, core_ids=list(range(B)))
    out = np.stack([res.results[b]["out"] for b in range(B)], axis=0)
    return out
